# version 03
# baseline (speedup 1.0000x reference)
import numpy as np
import ml_dtypes
from contextlib import ExitStack

import concourse.bass as bass
import concourse.mybir as mybir
from concourse.bass_utils import run_bass_kernel_spmd

F32 = mybir.dt.float32
BF16 = mybir.dt.bfloat16
I32 = mybir.dt.int32
AF = mybir.ActivationFunctionType
ALU = mybir.AluOpType
AX = mybir.AxisListType
BDT = ml_dtypes.bfloat16

NCORE = 8
S = 8192
SL = 1024
T = 8
E = 4096
EC = 32
D = 128
RSQD = float(1.0 / np.sqrt(128.0))
PI = float(np.pi)
TWO_PI_HI = float(np.float32(6.28125))
TWO_PI_LO = float(np.float32(2 * np.pi - 6.28125))
INV_2PI = float(np.float32(1.0 / (2 * np.pi)))
LN_EPS = 1e-5


class Sched:

    ENGINES = ("sync", "gpsimd", "tensor", "vector", "scalar")

    def __init__(self):
        self.ops = {e: [] for e in self.ENGINES}
        self.tot = {}
        self.sem_names = set()

    def inc(self, sem, n=1):
        self.sem_names.add(sem)
        self.tot[sem] = self.tot.get(sem, 0) + n
        return self.tot[sem]

    def cur(self, sem):
        self.sem_names.add(sem)
        return self.tot.get(sem, 0)

    def wait(self, eng, sem, val):
        if val <= 0:
            return
        self.sem_names.add(sem)
        self.ops[eng].append(lambda q, s, sem=sem, val=val: q.wait_ge(s[sem], val))

    def step(self, eng, fn, sem=None, n=1):
        if sem is None:
            self.ops[eng].append(fn)
            return None
        tot = self.inc(sem, n)
        self.ops[eng].append(lambda q, s, fn=fn, sem=sem, n=n: fn(q, s).then_inc(s[sem], n))
        return tot

    def dma(self, eng, dst, src, sem, **kw):
        tot = self.inc(sem, 16)
        self.ops[eng].append(
            lambda q, s, dst=dst, src=src, sem=sem, kw=kw: q.dma_start(out=dst, in_=src, **kw).then_inc(s[sem], 16))
        return tot

    def dma_t(self, eng, dst, src, sem):
        tot = self.inc(sem, 16)
        self.ops[eng].append(
            lambda q, s, dst=dst, src=src, sem=sem: q.dma_start_transpose(dst, src).then_inc(s[sem], 16))
        return tot

    def build(self, nc):
        with ExitStack() as es:
            sems = {name: es.enter_context(nc.semaphore(name)) for name in sorted(self.sem_names)}
            block = es.enter_context(nc.Block())

            @block.sync
            def _(q):
                for fn in self.ops["sync"]:
                    fn(q, sems)

            @block.gpsimd
            def _(q):
                for fn in self.ops["gpsimd"]:
                    fn(q, sems)

            @block.tensor
            def _(q):
                for fn in self.ops["tensor"]:
                    fn(q, sems)

            @block.vector
            def _(q):
                for fn in self.ops["vector"]:
                    fn(q, sems)

            @block.scalar
            def _(q):
                for fn in self.ops["scalar"]:
                    fn(q, sems)


def emit_rope_tables(nc, sch, es, pos_dram, pb):
    invf_np = (1.0 / 10000.0 ** (np.arange(0, D, 2, dtype=np.float64) / D)).astype(np.float32).reshape(1, 64)
    invf_c = nc.inline_tensor(invf_np, name="invf")
    invb = es.enter_context(nc.sbuf_tensor("rp_invb", [1, 64], F32))
    posb = es.enter_context(nc.sbuf_tensor("rp_posb", [1, SL], F32))
    ang = es.enter_context(nc.sbuf_tensor("rp_ang", [64, SL], F32))
    mtmp = es.enter_context(nc.sbuf_tensor("rp_mtmp", [64, SL], F32))
    mint = es.enter_context(nc.sbuf_tensor("rp_mint", [64, SL], I32))
    wred = es.enter_context(nc.sbuf_tensor("rp_wred", [64, SL], F32))
    shalf = es.enter_context(nc.sbuf_tensor("rp_half", [64, SL], F32))
    CC = es.enter_context(nc.sbuf_tensor("rp_CC", [128, SL], F32))
    SS = es.enter_context(nc.sbuf_tensor("rp_SS", [128, SL], F32))

    sch.dma("sync", invb[:], invf_c.ap()[:, :], "rp_ld")
    t_ld = sch.dma("sync", posb[:], pos_dram[:, :], "rp_ld")
    sch.wait("tensor", "rp_ld", t_ld)
    for half in range(2):
        t_mm = sch.step("tensor", lambda q, s, h=half: q.matmul(
            pb[0:64, 0, :], invb[:], posb[:, h * 512:(h + 1) * 512], start=True, stop=True), sem="rp_mm")
        sch.wait("vector", "rp_mm", t_mm)
        t_ev = sch.step("vector", lambda q, s, h=half: q.tensor_copy(
            ang[:, h * 512:(h + 1) * 512], pb[0:64, 0, :]), sem="rp_ve")
        sch.wait("tensor", "rp_ve", t_ev)

    def reduce_and_sin(shift):
        sch.step("vector", lambda q, s, shift=shift: q.tensor_scalar(
            out=wred[:], in0=ang[:], scalar1=float(shift), scalar2=None, op0=ALU.add))
        sch.step("vector", lambda q, s: q.tensor_scalar(
            out=mtmp[:], in0=wred[:], scalar1=INV_2PI, scalar2=None, op0=ALU.mult))
        sch.step("vector", lambda q, s: q.tensor_copy(mint[:], mtmp[:]))
        sch.step("vector", lambda q, s: q.tensor_copy(mtmp[:], mint[:]))
        sch.step("vector", lambda q, s: q.scalar_tensor_tensor(
            out=wred[:], in0=mtmp[:], scalar=-TWO_PI_HI, in1=wred[:], op0=ALU.mult, op1=ALU.add))
        sch.step("vector", lambda q, s: q.scalar_tensor_tensor(
            out=wred[:], in0=mtmp[:], scalar=-TWO_PI_LO, in1=wred[:], op0=ALU.mult, op1=ALU.add))
        sch.step("vector", lambda q, s: q.tensor_scalar(
            out=mtmp[:], in0=wred[:], scalar1=PI, scalar2=None, op0=ALU.is_gt))
        sch.step("vector", lambda q, s: q.scalar_tensor_tensor(
            out=wred[:], in0=mtmp[:], scalar=-2 * PI, in1=wred[:], op0=ALU.mult, op1=ALU.add))
        sch.step("vector", lambda q, s: q.tensor_scalar(
            out=mtmp[:], in0=wred[:], scalar1=-PI, scalar2=None, op0=ALU.is_lt))
        t_w = sch.step("vector", lambda q, s: q.scalar_tensor_tensor(
            out=wred[:], in0=mtmp[:], scalar=2 * PI, in1=wred[:], op0=ALU.mult, op1=ALU.add), sem="rp_ve")
        sch.wait("scalar", "rp_ve", t_w)
        t_s = sch.step("scalar", lambda q, s: q.activation(
            out=shalf[:], in_=wred[:], func=AF.Sin), sem="rp_act")
        sch.wait("vector", "rp_act", t_s)

    reduce_and_sin(0.0)
    sch.step("vector", lambda q, s: q.tensor_copy(SS[64:128, :], shalf[:]))
    t0 = sch.step("vector", lambda q, s: q.tensor_scalar(
        out=SS[0:64, :], in0=shalf[:], scalar1=-1.0, scalar2=None, op0=ALU.mult), sem="rp_ve")
    sch.wait("scalar", "rp_ve", t0)
    reduce_and_sin(PI / 2)
    sch.step("vector", lambda q, s: q.tensor_copy(CC[0:64, :], shalf[:]))
    sch.step("vector", lambda q, s: q.tensor_copy(CC[64:128, :], shalf[:]))
    return CC, SS


def build_A(nq_groups=8, do_k=True, do_v=True, do_x=True, simple_evac=False, skip_rope_tables=False):
    nc = bass.Bass("TRN2", target_bir_lowering=False)
    x = nc.dram_tensor("x", [SL, E], F32, kind="ExternalInput").ap()
    wq = nc.dram_tensor("wq", [E, E], F32, kind="ExternalInput").ap()
    bq = nc.dram_tensor("bq", [E], F32, kind="ExternalInput").ap()
    wk = nc.dram_tensor("wk", [E, 1024], F32, kind="ExternalInput").ap()
    bk = nc.dram_tensor("bk", [1024], F32, kind="ExternalInput").ap()
    wv = nc.dram_tensor("wv", [E, 1024], F32, kind="ExternalInput").ap()
    bv = nc.dram_tensor("bv", [1024], F32, kind="ExternalInput").ap()
    pos = nc.dram_tensor("pos", [1, SL], F32, kind="ExternalInput").ap()

    QT0 = nc.dram_tensor("QT0", [E, SL], BF16, kind="ExternalOutput").ap()
    K0T = nc.dram_tensor("K0T", [1024, SL], BF16, kind="ExternalOutput").ap()
    V0 = nc.dram_tensor("V0", [8, SL, 128], BF16, kind="ExternalOutput").ap()
    kn = nc.dram_tensor("kn", [1, 8], F32, kind="ExternalOutput").ap()

    xbf = nc.dram_tensor("xbf", [SL, E], BF16).ap()

    ones_bf = nc.inline_tensor(np.ones((1, 128), dtype=BDT), name="ones_bf")
    ones_f32 = nc.inline_tensor(np.ones((128, 1), dtype=np.float32), name="ones_f32")

    sch = Sched()
    with ExitStack() as es:
        pb = es.enter_context(nc.psum_tensor("pb", [128, 8, 512], F32))
        xT = es.enter_context(nc.sbuf_tensor("xT", [128, EC, SL], BF16))
        wt = es.enter_context(nc.sbuf_tensor("wt", [128, 2, 512], BF16))
        qf = es.enter_context(nc.sbuf_tensor("qf", [128, 512], F32))
        qsw = es.enter_context(nc.sbuf_tensor("qsw", [128, 512], F32))
        qrot = es.enter_context(nc.sbuf_tensor("qrot", [128, 512], F32))
        sqs = es.enter_context(nc.sbuf_tensor("sqs", [128, 512], F32))
        obuf = es.enter_context(nc.sbuf_tensor("obuf", [128, 4, SL], BF16))
        onesb = es.enter_context(nc.sbuf_tensor("onesb", [1, 128], BF16))
        onesf = es.enter_context(nc.sbuf_tensor("onesf", [128, 1], F32))
        bcol = es.enter_context(nc.sbuf_tensor("bcol", [128, 4], F32))
        bvrow = es.enter_context(nc.sbuf_tensor("bvrow", [1, 1024], BF16))
        nsc = es.enter_context(nc.sbuf_tensor("nsc", [1, 16], F32))
        knsb = es.enter_context(nc.sbuf_tensor("knsb", [1, 8], F32))
        vbuf = es.enter_context(nc.sbuf_tensor("vbuf", [128, 512], BF16))

        if skip_rope_tables:
            CC = es.enter_context(nc.sbuf_tensor("rp_CC", [128, SL], F32))
            SS = es.enter_context(nc.sbuf_tensor("rp_SS", [128, SL], F32))
            sch.step("vector", lambda q, s: q.memset(CC[:], 1.0))
            sch.step("vector", lambda q, s: q.memset(SS[:], 0.0))
        else:
            CC, SS = emit_rope_tables(nc, sch, es, pos, pb)

        sch.dma("gpsimd", onesb[:], ones_bf.ap()[:, :], "c_ld")
        sch.dma("gpsimd", onesf[:], ones_f32.ap()[:, :], "c_ld")
        sch.dma("gpsimd", bvrow[:], bv.rearrange("(a b) -> a b", a=1), "c_ld")
        if do_x:
            t_xc = sch.dma("gpsimd", xbf[:, :], x[:, :], "xc")
            sch.wait("sync", "xc", t_xc)
            for c in range(EC):
                sch.dma_t("sync", xT[:, c, :], xbf[:, c * 128:(c + 1) * 128], "xt")
        t_xt = sch.cur("xt")

        def proj_fm(w_dram, b_dram, out_dram, n_groups, do_norm):
            for g in range(n_groups):
                sch.wait("gpsimd", "bankfree", sch.cur("bankfree"))
                for hh in range(4):
                    sch.dma("gpsimd", bcol[:, hh:hh + 1],
                            b_dram[g * 512 + hh * 128: g * 512 + (hh + 1) * 128]
                            .rearrange("(p o) -> p o", o=1), "b_ld")
                t_b = sch.cur("b_ld")
                wbase = sch.cur("wfree")
                for ec in range(EC):
                    pp = ec % 2
                    sch.wait("gpsimd", "wfree", wbase + max(0, ec - 1) if ec >= 2 else wbase)
                    t_w = sch.dma("gpsimd", wt[:, pp, :],
                                  w_dram[ec * 128:(ec + 1) * 128, g * 512:(g + 1) * 512], "w_ld")
                    sch.wait("tensor", "w_ld", t_w)
                    if ec == 0:
                        sch.wait("tensor", "xt", t_xt)
                        sch.wait("tensor", "bankfree", sch.cur("bankfree"))
                        sch.wait("tensor", "nrd", sch.cur("nrd"))
                    for hh in range(4):
                        for qh in range(2):
                            bank = hh * 2 + qh
                            fn = lambda q, s, pp=pp, hh=hh, qh=qh, bank=bank, ec=ec: q.matmul(
                                pb[:, bank, :], wt[:, pp, hh * 128:(hh + 1) * 128],
                                xT[:, ec, qh * 512:(qh + 1) * 512],
                                start=(ec == 0), stop=(ec == EC - 1))
                            sch.step("tensor", fn, sem="wfree" if (hh == 3 and qh == 1) else None)
                sch.wait("vector", "wfree", wbase + EC)
                sch.wait("vector", "b_ld", t_b)
                for hh in range(4):
                    for qh in range(2):
                        bank = hh * 2 + qh
                        csl = (slice(0, 128), slice(qh * 512, (qh + 1) * 512))
                        if do_norm:
                            sch.wait("scalar", "nmm", sch.cur("nmm"))
                        if simple_evac:
                            sch.step("vector", lambda q, s, bank=bank: q.tensor_copy(
                                qf[:], pb[:, bank, :]), sem="bankfree")
                        else:
                            sch.step("vector", lambda q, s, bank=bank, hh=hh: q.scalar_tensor_tensor(
                                out=qf[:], in0=pb[:, bank, :], scalar=1.0,
                                in1=bcol[:, hh:hh + 1].to_broadcast([128, 512]),
                                op0=ALU.mult, op1=ALU.add), sem="bankfree")
                        if simple_evac:
                            t_r = sch.step("vector", lambda q, s: q.tensor_copy(qrot[:], qf[:]), sem="rope")
                            sch.wait("scalar", "rope", t_r)
                            t_c = sch.step("scalar", lambda q, s, hh=hh, qh=qh: q.copy(
                                obuf[:, hh, qh * 512:(qh + 1) * 512], qrot[:]), sem="cast")
                            sch.wait("vector", "cast", t_c)
                            continue
                        sch.step("vector", lambda q, s: q.tensor_copy(qsw[0:64, :], qf[64:128, :]))
                        sch.step("vector", lambda q, s: q.tensor_copy(qsw[64:128, :], qf[0:64, :]))
                        sch.step("vector", lambda q, s, csl=csl: q.tensor_tensor(
                            out=qrot[:], in0=qf[:], in1=CC[csl], op=ALU.mult))
                        sch.step("vector", lambda q, s, csl=csl: q.tensor_tensor(
                            out=qsw[:], in0=qsw[:], in1=SS[csl], op=ALU.mult))
                        t_r = sch.step("vector", lambda q, s: q.tensor_add(
                            out=qrot[:], in0=qrot[:], in1=qsw[:]), sem="rope")
                        sch.wait("scalar", "rope", t_r)
                        t_c = sch.step("scalar", lambda q, s, hh=hh, qh=qh: q.copy(
                            obuf[:, hh, qh * 512:(qh + 1) * 512], qrot[:]), sem="cast")
                        if do_norm:
                            t_sq = sch.step("scalar", lambda q, s: q.square(
                                sqs[:], qrot[:]), sem="cast")
                            sch.wait("tensor", "cast", t_sq)
                            t_n = sch.step("tensor", lambda q, s, bank=bank: q.matmul(
                                pb[0:1, bank, :], onesf[:], sqs[:], start=True, stop=True), sem="nmm")
                            sch.wait("vector", "nmm", t_n)
                            sch.step("vector", lambda q, s, hh=hh, qh=qh: q.reduce_max(
                                out=nsc[0:1, hh * 2 + qh: hh * 2 + qh + 1],
                                in_=pb[0:1, hh * 2 + qh, :], axis=AX.X), sem="nrd")
                            t_c = t_sq
                        sch.wait("vector", "cast", t_c)
                if do_norm:
                    for hh in range(4):
                        sch.step("vector", lambda q, s, g=g, hh=hh: q.reduce_max(
                            out=knsb[0:1, g * 4 + hh: g * 4 + hh + 1],
                            in_=nsc[0:1, hh * 2: hh * 2 + 2], axis=AX.X), sem="nrd")
                sch.wait("scalar", "cast", sch.cur("cast"))
                t_st = sch.dma("scalar",
                               out_dram[g * 512:(g + 1) * 512, :].rearrange("(hh p) q -> p hh q", p=128),
                               obuf[:, :, :], "store")
                sch.wait("scalar", "store", t_st)

        if nq_groups:
            proj_fm(wq, bq, QT0, nq_groups, do_norm=False)
        if do_k:
            proj_fm(wk, bk, K0T, 2, do_norm=True)
        if do_k:
            sch.wait("scalar", "nrd", sch.cur("nrd"))
            t_kn = sch.step("scalar", lambda q, s: q.sqrt(knsb[:, :], knsb[:, :]), sem="knd")
            sch.wait("scalar", "knd", t_kn)
            t_knst = sch.dma("scalar", kn[:, :], knsb[:, :], "store")

        for dvh in (range(2) if do_v else []):
            wbase = sch.cur("wfree")
            for ec in range(EC):
                pp = ec % 2
                sch.wait("gpsimd", "wfree", wbase + max(0, ec - 1) if ec >= 2 else wbase)
                t_w = sch.dma("gpsimd", wt[:, pp, :],
                              wv[ec * 128:(ec + 1) * 128, dvh * 512:(dvh + 1) * 512], "w_ld")
                sch.wait("tensor", "w_ld", t_w)
                if ec == 0:
                    sch.wait("tensor", "bankfree", sch.cur("bankfree"))
                    sch.wait("tensor", "c_ld", sch.cur("c_ld"))
                for kt in range(T):
                    fn = lambda q, s, pp=pp, kt=kt, ec=ec: q.matmul(
                        pb[:, kt, :], xT[:, ec, kt * 128:(kt + 1) * 128],
                        wt[:, pp, :], start=(ec == 0), stop=False)
                    sch.step("tensor", fn, sem="wfree" if kt == T - 1 else None)
            for kt in range(T):
                tm = sch.step("tensor", lambda q, s, kt=kt, dvh=dvh: q.matmul(
                    pb[:, kt, :], onesb[:], bvrow[:, dvh * 512:(dvh + 1) * 512],
                    start=False, stop=True), sem="mmg" if kt == T - 1 else None)
            sch.wait("scalar", "mmg", tm)
            for kt in range(T):
                t_c = sch.step("scalar", lambda q, s, kt=kt: q.copy(
                    vbuf[:], pb[:, kt, :]), sem="bankfree")
                sch.wait("scalar", "bankfree", t_c)
                for hh in range(4):
                    t_st = sch.dma("scalar", V0[dvh * 4 + hh, kt * 128:(kt + 1) * 128, :],
                                   vbuf[:, hh * 128:(hh + 1) * 128], "store")
                sch.wait("scalar", "store", t_st)

        sch.wait("sync", "store", sch.cur("store"))
        sch.wait("sync", "knd", sch.cur("knd"))
        sch.build(nc)
    return nc


def _shard_rows(a, i):
    return np.ascontiguousarray(a[i::NCORE])


def make_inmaps_A(inputs):
    x = np.asarray(inputs["L0_attn_x"])[0]
    in_maps = []
    for i in range(NCORE):
        in_maps.append({
            "x": _shard_rows(x, i),
            "wq": np.asarray(inputs["L0_attn_wq"]),
            "bq": np.asarray(inputs["L0_attn_bq"]),
            "wk": np.asarray(inputs["L0_attn_wk"]),
            "bk": np.asarray(inputs["L0_attn_bk"]),
            "wv": np.asarray(inputs["L0_attn_wv"]),
            "bv": np.asarray(inputs["L0_attn_bv"]),
            "pos": np.arange(i, S, NCORE, dtype=np.float32).reshape(1, SL),
        })
    return in_maps


def run_A(inputs):
    nc = build_A()
    res = run_bass_kernel_spmd(nc, make_inmaps_A(inputs), core_ids=list(range(NCORE)))
    return res.results



def build_BC(do_attn=True, do_ffn=True, do_qkv1=True, skip_ln=False):
    nc = bass.Bass("TRN2", target_bir_lowering=False)
    x = nc.dram_tensor("x", [SL, E], F32, kind="ExternalInput").ap()
    QT0d = nc.dram_tensor("QT0", [E, SL], BF16, kind="ExternalInput").ap()
    K0Td = nc.dram_tensor("K0T", [1024, S], BF16, kind="ExternalInput").ap()
    V0d = nc.dram_tensor("V0", [8, S, 128], BF16, kind="ExternalInput").ap()
    knall = nc.dram_tensor("knall", [1, 64], F32, kind="ExternalInput").ap()
    A_w = nc.dram_tensor("A_w", [512, E], F32, kind="ExternalInput").ap()
    B_w = nc.dram_tensor("B_w", [E, 512], F32, kind="ExternalInput").ap()
    Wq_w = nc.dram_tensor("Wq_w", [16, E], F32, kind="ExternalInput").ap()
    keys = nc.dram_tensor("keys", [64, 16], F32, kind="ExternalInput").ap()
    wq1 = nc.dram_tensor("wq1", [E, E], F32, kind="ExternalInput").ap()
    wk1 = nc.dram_tensor("wk1", [E, 128], F32, kind="ExternalInput").ap()
    wv1 = nc.dram_tensor("wv1", [E, 128], F32, kind="ExternalInput").ap()

    x2o = nc.dram_tensor("x2", [SL, E], F32, kind="ExternalOutput").ap()
    QT1o = nc.dram_tensor("QT1", [E, SL], BF16, kind="ExternalOutput").ap()
    K1o = nc.dram_tensor("K1", [SL, 128], BF16, kind="ExternalOutput").ap()
    V1o = nc.dram_tensor("V1", [SL, 128], BF16, kind="ExternalOutput").ap()
    kn1o = nc.dram_tensor("kn1", [1, 1], F32, kind="ExternalOutput").ap()

    x1bf = nc.dram_tensor("x1bf", [SL, E], BF16).ap()
    zgbf = nc.dram_tensor("zgbf", [SL, 512], BF16).ap()
    x2bf = nc.dram_tensor("x2bf", [SL, E], BF16).ap()
    Abf = nc.dram_tensor("Abf", [512, E], BF16).ap()
    Bbf = nc.dram_tensor("Bbf", [E, 512], BF16).ap()

    ones_f32 = nc.inline_tensor(np.ones((128, 1), dtype=np.float32), name="ones_f32")
    onesrow_f = nc.inline_tensor(np.ones((1, 128), dtype=np.float32), name="onesrow_f")
    ident_f = nc.inline_tensor(np.eye(128, dtype=np.float32), name="ident_f")

    sch = Sched()
    with ExitStack() as es0:
        pb = es0.enter_context(nc.psum_tensor("pb", [128, 8, 512], F32))
        XR = es0.enter_context(nc.sbuf_tensor("XR", [128, T, E], F32))
        msc = es0.enter_context(nc.sbuf_tensor("msc", [128, T, 32], F32))
        onesf = es0.enter_context(nc.sbuf_tensor("onesf", [128, 1], F32))
        onesrf = es0.enter_context(nc.sbuf_tensor("onesrf", [1, 128], F32))
        identf = es0.enter_context(nc.sbuf_tensor("identf", [128, 128], F32))
        negBb = es0.enter_context(nc.sbuf_tensor("negBb", [128, 32], F32))
        qnr = es0.enter_context(nc.sbuf_tensor("qnr", [1, 32], F32))
        kmaxr = es0.enter_context(nc.sbuf_tensor("kmaxr", [1, 64], F32))
        ln1sc = es0.enter_context(nc.sbuf_tensor("ln1sc", [128, 8], F32))
        rzs = es0.enter_context(nc.sbuf_tensor("rzs", [128, 8], F32))

        sch.dma("sync", onesf[:], ones_f32.ap()[:, :], "cl")
        sch.dma("sync", onesrf[:], onesrow_f.ap()[:, :], "cl")
        sch.dma("sync", identf[:], ident_f.ap()[:, :], "cl")
        sch.dma("sync", kmaxr[:], knall[:, :], "cl")
        for t in range(T):
            sch.dma("sync", XR[:, t, :], x[t * 128:(t + 1) * 128, :], "xld")
        t_xld = sch.cur("xld")

        def emit_ln(tile, ncols, lnsem, junk):
            sch.step("vector", lambda q, s, tile=tile, ncols=ncols: q.reduce_sum(
                out=ln1sc[:, tile:tile + 1], in_=msc[:, tile, 0:ncols], axis=AX.X))
            sch.step("vector", lambda q, s: q.drain())
            t_nm = sch.step("vector", lambda q, s, tile=tile: q.tensor_scalar(
                out=ln1sc[:, tile:tile + 1], in0=ln1sc[:, tile:tile + 1],
                scalar1=-1.0 / E, scalar2=None, op0=ALU.mult), sem=lnsem)
            sch.wait("scalar", lnsem, t_nm)
            t_sq = sch.step("scalar", lambda q, s, tile=tile, junk=junk: q.activation(
                out=junk, in_=XR[:, tile, :], func=AF.Square,
                bias=ln1sc[:, tile:tile + 1], accum_out=pbq(tile)), sem=lnsem)
            sch.wait("vector", lnsem, t_sq)
            t_ve = sch.step("vector", lambda q, s, tile=tile: q.tensor_scalar(
                out=pbq(tile), in0=pbq(tile), scalar1=1.0 / E, scalar2=LN_EPS,
                op0=ALU.mult, op1=ALU.add), sem=lnsem)
            sch.wait("scalar", lnsem, t_ve)
            t_va = sch.step("scalar", lambda q, s, tile=tile: q.activation(
                out=pbs(tile), in_=pbq(tile), func=AF.Sqrt), sem=lnsem)
            sch.wait("vector", lnsem, t_va)
            sch.step("vector", lambda q, s: q.drain())
            sch.step("vector", lambda q, s, tile=tile: q.reciprocal(
                out=pbr(tile), in_=pbs(tile)))
            sch.step("vector", lambda q, s: q.drain())
            sch.step("vector", lambda q, s, tile=tile: q.scalar_tensor_tensor(
                out=pbm(tile), in0=ln1sc[:, tile:tile + 1], scalar=1.0,
                in1=pbr(tile), op0=ALU.mult, op1=ALU.mult))
            sch.step("vector", lambda q, s: q.drain())
            t_ln = sch.step("vector", lambda q, s, tile=tile: q.scalar_tensor_tensor(
                out=XR[:, tile, :], in0=XR[:, tile, :], scalar=pbr(tile),
                in1=pbm(tile).to_broadcast([128, E]), op0=ALU.mult, op1=ALU.add), sem=lnsem)
            return t_ln

        lnsl = es0.enter_context(nc.sbuf_tensor("lnsl", [128, T, 4], F32))
        def pbq(t):
            return lnsl[:, t, 0:1]
        def pbs(t):
            return lnsl[:, t, 1:2]
        def pbr(t):
            return lnsl[:, t, 2:3]
        def pbm(t):
            return lnsl[:, t, 3:4]

        if do_attn:
            with ExitStack() as es1:
                QTg = es1.enter_context(nc.sbuf_tensor("QTg", [128, 2, 2, SL], BF16))
                Kb = es1.enter_context(nc.sbuf_tensor("Kb", [128, 2, 4096], BF16))
                Vb = es1.enter_context(nc.sbuf_tensor("Vb", [128, 2, 32, 130], BF16))
                Esb = es1.enter_context(nc.sbuf_tensor("Esb", [128, 2, 2, 512], BF16))
                qnb = es1.enter_context(nc.sbuf_tensor("qnb", [128, 2, SL], BF16))
                sqs = es1.enter_context(nc.sbuf_tensor("sqs", [128, SL], F32))

                for h in range(32):
                    pp = h % 2
                    if h >= 2:
                        sch.wait("sync", "qnf", h - 1)
                    t_l = sch.dma("sync", qnb[:, pp, :], QT0d[h * 128:(h + 1) * 128, :], "qnl")
                    sch.wait("scalar", "qnl", t_l)
                    t_s = sch.step("scalar", lambda q, s, pp=pp: q.square(
                        sqs[:, :], qnb[:, pp, :]), sem="qns")
                    sch.wait("tensor", "qns", t_s)
                    for half in range(2):
                        t_m = sch.step("tensor", lambda q, s, half=half, h=h: q.matmul(
                            pb[0:1, 7, half * 64:half * 64 + 512] if False else pb[0:1, half, :],
                            onesf[:], sqs[:, half * 512:(half + 1) * 512],
                            start=True, stop=True), sem="qnm" if half == 1 else None)
                    sch.wait("vector", "qnm", sch.cur("qnm"))
                    sch.step("vector", lambda q, s: q.reduce_max(
                        out=lnsl[0:1, 0, 0:1], in_=pb[0:1, 0, :], axis=AX.X))
                    sch.step("vector", lambda q, s: q.reduce_max(
                        out=lnsl[0:1, 0, 1:2], in_=pb[0:1, 1, :], axis=AX.X))
                    t_f = sch.step("vector", lambda q, s, h=h: q.reduce_max(
                        out=qnr[0:1, h:h + 1], in_=lnsl[0:1, 0, 0:2], axis=AX.X), sem="qnf")
                    sch.wait("tensor", "qnf", t_f)
                sch.wait("scalar", "qnf", sch.cur("qnf"))
                t_sq2 = sch.step("scalar", lambda q, s: q.sqrt(qnr[:, :], qnr[:, :]), sem="qns")
                sch.wait("vector", "qns", t_sq2)
                sch.wait("vector", "cl", sch.cur("cl"))
                sch.step("vector", lambda q, s: q.reduce_max(
                    out=kmaxr[0:1, 0:8], in_=kmaxr[:, :].rearrange("a (r h) -> a h r", h=8), axis=AX.X))
                t_nb = sch.step("vector", lambda q, s: q.scalar_tensor_tensor(
                    out=qnr[:, :], in0=qnr[:, :].rearrange("a (g hh) -> a g hh", g=8), scalar=-RSQD,
                    in1=kmaxr[0:1, 0:8].rearrange("a (g o) -> a g o", o=1).to_broadcast([1, 8, 4]),
                    op0=ALU.mult, op1=ALU.mult), sem="qnf")
                sch.wait("tensor", "qnf", sch.cur("qnf"))
                t_bb = sch.step("tensor", lambda q, s: q.matmul(
                    pb[:, 0, 0:32], onesrf[:], qnr[:, :], start=True, stop=True), sem="qnm")
                sch.wait("vector", "qnm", sch.cur("qnm"))
                t_ev = sch.step("vector", lambda q, s: q.tensor_copy(negBb[:, :], pb[:, 0, 0:32]), sem="qnf")
                sch.wait("tensor", "qnf", sch.cur("qnf"))
                sch.wait("scalar", "qnf", sch.cur("qnf"))

                ACS = 132
                def pacc(i):
                    return pb[:, 4 + i // 3, (i % 3) * ACS:(i % 3) * ACS + 129]
                sch.wait("vector", "xld", t_xld)
                pi = 0
                for g in range(8):
                    for hp in range(2):
                        npq = g * 2 + hp
                        ppq = npq % 2
                        h0 = g * 4 + hp * 2
                        if npq >= 2:
                            sch.wait("sync", "avd", 128 * (npq - 1))
                        t_qg = sch.dma("sync", QTg[:, ppq, :, :],
                                       QT0d[h0 * 128:(h0 + 2) * 128, :].rearrange("(c p) q -> p c q", p=128),
                                       "qgl")
                        for qh in range(2):
                            if pi > 0:
                                sch.wait("sync", "avd", 64 * pi)
                            for khl in range(2):
                                t_k = sch.dma("sync", Kb[:, khl, :],
                                              K0Td[g * 128:(g + 1) * 128, khl * 4096:(khl + 1) * 4096], "kvl")
                                for part in range(4):
                                    sch.dma("sync", Vb[:, khl, part * 8:(part + 1) * 8, 0:128],
                                            V0d[g, khl * 4096 + part * 1024:khl * 4096 + (part + 1) * 1024,
                                                :].rearrange("(kc p) d -> p kc d", p=128),
                                            "kvl")
                            t_kv1 = sch.cur("kvl")
                            if pi == 0:
                                sch.wait("vector", "kvl", t_kv1)
                                sch.step("vector", lambda q, s: q.memset(Vb[:, 0, :, 128:129], 1.0))
                                t_vo = sch.step("vector", lambda q, s: q.memset(
                                    Vb[:, 1, :, 128:129], 1.0), sem="vone")
                                sch.wait("tensor", "vone", t_vo)
                            base_eex = sch.cur("eex")
                            base_avd = sch.cur("avd")
                            base_smm = sch.cur("smm")
                            for kcg in range(65):
                                if kcg < 64:
                                    kh, kc = kcg // 32, kcg % 32
                                    if kc == 0:
                                        sch.wait("tensor", "kvl", t_kv1 - (1 - kh) * 5 * 16)
                                        if kcg == 0:
                                            sch.wait("tensor", "qgl", t_qg)
                                            sch.wait("tensor", "evd", 8 * pi)
                                            sch.wait("tensor", "vone", sch.cur("vone"))
                                    if kcg >= 2:
                                        sch.wait("tensor", "eex", base_eex + 2 * (kcg - 1))
                                    for hh in range(2):
                                        sch.step("tensor", lambda q, s, hh=hh, kh=kh, kc=kc, kcg=kcg, ppq=ppq, qh=qh: q.matmul(
                                            pb[:, hh * 2 + kcg % 2, :],
                                            Kb[:, kh, kc * 128:(kc + 1) * 128],
                                            QTg[:, ppq, hh, qh * 512:(qh + 1) * 512],
                                            start=True, stop=True),
                                            sem="smm")
                                if kcg >= 1:
                                    pk = kcg - 1
                                    pkh = pk // 32
                                    sch.wait("tensor", "eex", base_eex + 2 * kcg)
                                    for idx in range(8):
                                        hh, qt = idx // 4, idx % 4
                                        sch.step("tensor", lambda q, s, pk=pk, pkh=pkh, idx=idx, hh=hh, qt=qt: q.matmul(
                                            pacc(idx),
                                            Esb[:, pk % 2, hh, qt * 128:(qt + 1) * 128],
                                            Vb[:, pkh, pk % 32, 0:129],
                                            start=(pk == 0 and idx % 3 == 0), stop=(pk == 63),
                                            skip_group_check=True),
                                            sem="avd" if idx == 7 else None)
                            for kcg in range(64):
                                if kcg == 0:
                                    sch.wait("scalar", "avd", base_avd)
                                if kcg >= 2:
                                    sch.wait("scalar", "avd", base_avd + kcg - 1)
                                sch.wait("scalar", "smm", base_smm + 2 * kcg + 2)
                                for hh in range(2):
                                    hd = h0 + hh
                                    sch.step("scalar", lambda q, s, hh=hh, kcg=kcg, hd=hd: q.activation(
                                        out=Esb[:, kcg % 2, hh, :], in_=pb[:, hh * 2 + kcg % 2, :],
                                        func=AF.Exp, bias=negBb[:, hd:hd + 1], scale=RSQD),
                                        sem="eex")
                            sch.wait("vector", "avd", base_avd + 64)
                            sch.step("vector", lambda q, s: q.drain())
                            for idx in range(8):
                                sch.step("vector", lambda q, s, idx=idx: q.reciprocal(
                                    out=rzs[:, idx:idx + 1], in_=pacc(idx)[:, 128:129]))
                            sch.step("vector", lambda q, s: q.drain())
                            for idx in range(8):
                                hh, qt = idx // 4, idx % 4
                                hd = h0 + hh
                                tg = qh * 4 + qt
                                sch.step("vector", lambda q, s, idx=idx, tg=tg, hd=hd: q.scalar_tensor_tensor(
                                    out=XR[:, tg, hd * 128:(hd + 1) * 128], in0=pacc(idx)[:, 0:128],
                                    scalar=rzs[:, idx:idx + 1], in1=XR[:, tg, hd * 128:(hd + 1) * 128],
                                    op0=ALU.mult, op1=ALU.add,
                                    accum_out=msc[:, tg, hd:hd + 1]), sem="evd")
                            pi += 1
                for t in range(T):
                    emit_ln(t, 32, "ln1", Kb[:, 0, :])

        es2 = es0.enter_context(ExitStack())
        xT = es2.enter_context(nc.sbuf_tensor("xT", [128, EC, SL], BF16))
        wt2 = es2.enter_context(nc.sbuf_tensor("wt2", [128, 2, 512], BF16))

        def cast_store_stream(dst_dram, stsem):
            stg = xT[:, 0:4, :].rearrange("p c q -> p (c q)")
            for t in range(T):
                t_c = sch.step("vector", lambda q, s, t=t, stg=stg: q.tensor_copy(
                    stg[:, 0:E], XR[:, t, :]), sem="xcst")
                sch.wait("sync", "xcst", t_c)
                t_s = sch.dma("sync", dst_dram[t * 128:(t + 1) * 128, :], stg[:, 0:E], stsem)
                sch.wait("vector", stsem, t_s)
            return sch.cur(stsem)

        def load_xT(src_dram):
            for c in range(EC):
                sch.dma_t("sync", xT[:, c, :], src_dram[:, c * 128:(c + 1) * 128], "xtl")
            return sch.cur("xtl")

        if do_ffn:
            Wqbfd = nc.dram_tensor("Wqbfd", [16, E], BF16).ap()
            sch.dma("gpsimd", Abf[:, :], A_w[:, :], "wcst")
            sch.dma("gpsimd", Bbf[:, :], B_w[:, :], "wcst")
            sch.dma("gpsimd", Wqbfd[:, :], Wq_w[:, :], "wcst")

            sch.wait("vector", "ln1", sch.cur("ln1"))
            t_x1w = cast_store_stream(x1bf, "x1st")
            sch.wait("sync", "x1st", t_x1w)
            t_xt = load_xT(x1bf)

            es2b = es0.enter_context(ExitStack())
            WqTb = es2b.enter_context(nc.sbuf_tensor("WqTb", [128, EC, 16], BF16))
            keyssb = es2b.enter_context(nc.sbuf_tensor("keyssb", [64, 16], F32))
            keysTb = es2b.enter_context(nc.sbuf_tensor("keysTb", [16, 64], F32))
            qrsb = es2b.enter_context(nc.sbuf_tensor("qrsb", [128, 128], F32))
            qrTb = es2b.enter_context(nc.sbuf_tensor("qrTb", [16, 2, 128], F32))
            s2b = es2b.enter_context(nc.sbuf_tensor("s2b", [128, 2, 64], F32))
            s2w = es2b.enter_context(nc.sbuf_tensor("s2w", [128, 64], F32))
            m8b = es2b.enter_context(nc.sbuf_tensor("m8b", [128, 16], F32))
            tksc = es2b.enter_context(nc.sbuf_tensor("tksc", [128, 4], F32))
            gates = es2b.enter_context(nc.sbuf_tensor("gates", [128, T, 64], F32))
            zgsb = es2b.enter_context(nc.sbuf_tensor("zgsb", [128, 2, 512], BF16))

            sch.wait("sync", "wcst", sch.cur("wcst"))
            for c in range(EC):
                sch.dma_t("sync", WqTb[:, c, :], Wqbfd[0:16, c * 128:(c + 1) * 128], "wql")
            t_wql = sch.cur("wql")
            t_ks = sch.dma("sync", keyssb[:], keys[:, :], "wql")
            sch.wait("tensor", "wql", sch.cur("wql"))
            t_kt = sch.step("tensor", lambda q, s: q.transpose(
                pb[0:16, 1, 0:64], keyssb[:], identf[0:64, 0:64]), sem="rmm")
            sch.wait("vector", "rmm", t_kt)
            t_ktc = sch.step("vector", lambda q, s: q.tensor_copy(
                keysTb[:], pb[0:16, 1, 0:64]), sem="rve")

            sch.wait("tensor", "xtl", t_xt)
            sch.wait("tensor", "rve", t_ktc)
            for qt in range(T):
                for ec in range(EC):
                    sch.step("tensor", lambda q, s, qt=qt, ec=ec: q.matmul(
                        pb[:, 0, qt * 16:(qt + 1) * 16],
                        xT[:, ec, qt * 128:(qt + 1) * 128], WqTb[:, ec, :],
                        start=(qt == 0 and ec == 0), stop=(ec == EC - 1),
                        skip_group_check=True),
                        sem="rmm" if (ec == EC - 1 and qt == T - 1) else None)
            sch.wait("vector", "rmm", sch.cur("rmm"))
            t_qr = sch.step("vector", lambda q, s: q.tensor_copy(
                qrsb[:, :], pb[:, 0, 0:128]), sem="rve")
            sch.wait("tensor", "rve", t_qr)
            for qt in range(T):
                t_tr = sch.step("tensor", lambda q, s, qt=qt: q.transpose(
                    pb[0:16, 2 + qt % 2, 0:128], qrsb[:, qt * 16:(qt + 1) * 16],
                    identf[:, :]), sem="rmm")
                sch.wait("vector", "rmm", t_tr)
                t_qtc = sch.step("vector", lambda q, s, qt=qt: q.tensor_copy(
                    qrTb[:, qt % 2, :], pb[0:16, 2 + qt % 2, 0:128]), sem="rve")
                sch.wait("tensor", "rve", t_qtc)
                t_s2 = sch.step("tensor", lambda q, s, qt=qt: q.matmul(
                    pb[:, 4 + qt % 2, 0:64], qrTb[:, qt % 2, :], keysTb[:, :],
                    start=True, stop=True), sem="rmm")
                sch.wait("vector", "rmm", t_s2)
                t_s2c = sch.step("vector", lambda q, s, qt=qt: q.tensor_copy(
                    s2b[:, qt % 2, :], pb[:, 4 + qt % 2, 0:64]), sem="rve")
                sch.wait("tensor", "rve", t_s2c)
                sv = s2b[:, qt % 2, :]
                sch.step("vector", lambda q, s, sv=sv: q.tensor_copy(s2w[:], sv))
                sch.step("vector", lambda q, s: q.max(out=m8b[:, 0:8], in_=s2w[:]))
                sch.step("vector", lambda q, s: q.drain())
                sch.step("vector", lambda q, s: q.match_replace(
                    out=s2w[:], in_to_replace=m8b[:, 0:8], in_values=s2w[:], imm_value=-1e30))
                sch.step("vector", lambda q, s: q.max(out=m8b[:, 8:16], in_=s2w[:]))
                sch.step("vector", lambda q, s: q.drain())
                sch.step("vector", lambda q, s: q.tensor_scalar(
                    out=tksc[:, 0:1], in0=m8b[:, 0:1], scalar1=-1.0, scalar2=None, op0=ALU.mult))
                t_tk = sch.step("vector", lambda q, s: q.tensor_copy(
                    tksc[:, 2:3], m8b[:, 15:16]), sem="tk")
                sch.wait("scalar", "tk", t_tk)
                sch.step("scalar", lambda q, s: q.activation(
                    out=m8b[:, 0:16], in_=m8b[:, 0:16], func=AF.Exp,
                    bias=tksc[:, 0:1], accum_out=tksc[:, 1:2]))
                t_ea = sch.step("scalar", lambda q, s, sv=sv: q.activation(
                    out=s2w[:], in_=sv, func=AF.Exp, bias=tksc[:, 0:1]), sem="tk")
                sch.wait("vector", "tk", t_ea)
                sch.step("vector", lambda q, s: q.drain())
                sch.step("vector", lambda q, s, qt=qt, sv=sv: q.tensor_scalar(
                    out=gates[:, qt, :], in0=sv, scalar1=tksc[:, 2:3], scalar2=None,
                    op0=ALU.is_ge))
                sch.step("vector", lambda q, s: q.reciprocal(
                    out=tksc[:, 3:4], in_=tksc[:, 1:2]))
                sch.step("vector", lambda q, s, qt=qt: q.tensor_tensor(
                    out=gates[:, qt, :], in0=gates[:, qt, :], in1=s2w[:], op=ALU.mult))
                sch.step("vector", lambda q, s: q.drain())
                sch.step("vector", lambda q, s, qt=qt: q.tensor_scalar(
                    out=gates[:, qt, :], in0=gates[:, qt, :], scalar1=tksc[:, 3:4],
                    scalar2=None, op0=ALU.mult))
                sch.step("vector", lambda q, s, qt=qt: q.tensor_scalar(
                    out=gates[:, qt, :], in0=gates[:, qt, :], scalar1=2.0,
                    scalar2=None, op0=ALU.mult), sem="gdone")
            t_gd = sch.cur("gdone")

            zb = sch.cur("zmm")
            for ec in range(EC):
                pp = ec % 2
                if ec >= 2:
                    sch.wait("sync", "zmm", zb + ec - 1)
                t_w = sch.dma_t("sync", wt2[:, pp, :], Abf[0:512, ec * 128:(ec + 1) * 128], "atl")
                sch.wait("tensor", "atl", t_w)
                if ec == 0:
                    sch.wait("tensor", "rve", sch.cur("rve"))
                for qt in range(T):
                    sch.step("tensor", lambda q, s, qt=qt, ec=ec, pp=pp: q.matmul(
                        pb[:, qt, :], xT[:, ec, qt * 128:(qt + 1) * 128], wt2[:, pp, :],
                        start=(ec == 0), stop=(ec == EC - 1)),
                        sem="zmm" if qt == T - 1 else None)
            sch.wait("vector", "zmm", sch.cur("zmm"))
            sch.wait("vector", "gdone", t_gd)
            for qt in range(T):
                t_zg = sch.step("vector", lambda q, s, qt=qt: q.tensor_tensor(
                    out=zgsb[:, qt % 2, :].rearrange("p (e r) -> p e r", r=8),
                    in0=pb[:, qt, :].rearrange("p (e r) -> p e r", r=8),
                    in1=gates[:, qt, :].unsqueeze(2).to_broadcast([128, 64, 8]),
                    op=ALU.mult), sem="zgd")
                sch.wait("sync", "zgd", t_zg)
                t_zst = sch.dma("sync", zgbf[qt * 128:(qt + 1) * 128, :], zgsb[:, qt % 2, :], "zst")
                if qt >= 1:
                    sch.wait("vector", "zst", t_zst - 16)
            es2b.close()

            es2c = es0.enter_context(ExitStack())
            zgb = es2c.enter_context(nc.sbuf_tensor("zgb", [128, 2, SL], BF16))
            sch.wait("sync", "zst", sch.cur("zst"))
            for o in range(8):
                for ncb in range(4):
                    step = o * 4 + ncb
                    pp = step % 2
                    if step >= 2:
                        sch.wait("sync", "umm", step - 1)
                    t_bt = sch.dma_t("sync", wt2[:, pp, :],
                                     Bbf[o * 512:(o + 1) * 512, ncb * 128:(ncb + 1) * 128], "btl")
                    t_zg2 = sch.dma_t("sync", zgb[:, pp, :],
                                      zgbf[:, ncb * 128:(ncb + 1) * 128], "btl")
                    sch.wait("tensor", "btl", t_zg2)
                    if ncb == 0:
                        sch.wait("tensor", "uev", 8 * o)
                    for qt in range(T):
                        sch.step("tensor", lambda q, s, qt=qt, ncb=ncb, pp=pp: q.matmul(
                            pb[:, qt, :], zgb[:, pp, qt * 128:(qt + 1) * 128], wt2[:, pp, :],
                            start=(ncb == 0), stop=(ncb == 3)),
                            sem="umm" if qt == T - 1 else None)
                sch.wait("vector", "umm", sch.cur("umm"))
                for qt in range(T):
                    sch.step("vector", lambda q, s, qt=qt, o=o: q.scalar_tensor_tensor(
                        out=XR[:, qt, o * 512:(o + 1) * 512], in0=pb[:, qt, :],
                        scalar=1.0, in1=XR[:, qt, o * 512:(o + 1) * 512],
                        op0=ALU.mult, op1=ALU.add,
                        accum_out=msc[:, qt, o:o + 1]), sem="uev")
            es2c.close()
            for t in range(T):
                emit_ln(t, 8, "ln2", xT[:, 2:6, :].rearrange("p c q -> p (c q)"))
            sch.wait("vector", "ln2", sch.cur("ln2"))

        if do_qkv1:
            es3 = es0.enter_context(ExitStack())
            obuf3 = es3.enter_context(nc.sbuf_tensor("obuf3", [128, 4, SL], BF16))
            sq3 = es3.enter_context(nc.sbuf_tensor("sq3", [128, 128], F32))
            kn1sb = es3.enter_context(nc.sbuf_tensor("kn1sb", [1, 4], F32))
            kncl = es3.enter_context(nc.sbuf_tensor("kncl", [1, T, 128], BF16))
            t_x2w = cast_store_stream(x2bf, "x2st")
            for t in range(T):
                sch.dma("sync", x2o[t * 128:(t + 1) * 128, :], XR[:, t, :], "x2st")
            t_x2o = sch.cur("x2st")
            sch.wait("sync", "x2st", t_x2o)
            t_xt2 = load_xT(x2bf)

            sch.wait("gpsimd", "ln2", sch.cur("ln2"))
            for g in range(8):
                wb3 = sch.cur("w3f")
                for ec in range(EC):
                    pp = ec % 2
                    sch.wait("gpsimd", "w3f", (wb3 + ec - 1) if ec >= 2 else wb3)
                    t_w = sch.dma("gpsimd", wt2[:, pp, :],
                                  wq1[ec * 128:(ec + 1) * 128, g * 512:(g + 1) * 512], "w3l")
                    sch.wait("tensor", "w3l", t_w)
                    if ec == 0:
                        sch.wait("tensor", "xtl", t_xt2)
                        sch.wait("tensor", "q1e", sch.cur("q1e"))
                    for hh in range(4):
                        for qh in range(2):
                            sch.step("tensor", lambda q, s, pp=pp, hh=hh, qh=qh, ec=ec: q.matmul(
                                pb[:, hh * 2 + qh, :], wt2[:, pp, hh * 128:(hh + 1) * 128],
                                xT[:, ec, qh * 512:(qh + 1) * 512],
                                start=(ec == 0), stop=(ec == EC - 1)),
                                sem="w3f" if (hh == 3 and qh == 1) else None)
                sch.wait("scalar", "w3f", sch.cur("w3f"))
                for hh in range(4):
                    for qh in range(2):
                        sch.step("scalar", lambda q, s, hh=hh, qh=qh: q.copy(
                            obuf3[:, hh, qh * 512:(qh + 1) * 512], pb[:, hh * 2 + qh, :]),
                            sem="q1e")
                sch.wait("scalar", "q1e", sch.cur("q1e"))
                t_st = sch.dma("scalar",
                               QT1o[g * 512:(g + 1) * 512, :].rearrange("(hh p) q -> p hh q", p=128),
                               obuf3[:, :, :], "q1st")
                sch.wait("scalar", "q1st", t_st)

            wb3 = sch.cur("w3f")
            for ec in range(EC):
                pp = ec % 2
                sch.wait("gpsimd", "w3f", (wb3 + ec - 1) if ec >= 2 else wb3)
                sch.dma("gpsimd", wt2[:, pp, 0:128], wk1[ec * 128:(ec + 1) * 128, :], "w3l")
                t_w = sch.dma("gpsimd", wt2[:, pp, 128:256], wv1[ec * 128:(ec + 1) * 128, :], "w3l")
                sch.wait("tensor", "w3l", t_w)
                if ec == 0:
                    sch.wait("tensor", "q1e", sch.cur("q1e"))
                for kt in range(T):
                    sch.step("tensor", lambda q, s, pp=pp, kt=kt, ec=ec: q.matmul(
                        pb[:, kt, 0:256], xT[:, ec, kt * 128:(kt + 1) * 128],
                        wt2[:, pp, 0:256],
                        start=(ec == 0), stop=(ec == EC - 1)),
                        sem="w3f" if kt == T - 1 else None)
            sch.wait("scalar", "w3f", sch.cur("w3f"))
            sch.wait("vector", "w3f", sch.cur("w3f"))
            for kt in range(T):
                sch.step("scalar", lambda q, s, kt=kt: q.copy(
                    obuf3[:, 0, kt * 128:(kt + 1) * 128], pb[:, kt, 0:128]), sem="q1e")
                t_vc = sch.step("scalar", lambda q, s, kt=kt: q.copy(
                    obuf3[:, 1, kt * 128:(kt + 1) * 128], pb[:, kt, 128:256]), sem="q1e")
                t_sq = sch.step("scalar", lambda q, s, kt=kt: q.square(
                    sq3[:, 0:128], pb[:, kt, 0:128]), sem="q1e")
                sch.wait("tensor", "q1e", t_sq)
                t_nm = sch.step("tensor", lambda q, s, kt=kt: q.matmul(
                    pb[0:1, kt, 256:384], onesf[:], sq3[:, 0:128],
                    start=True, stop=True, skip_group_check=True), sem="nm1")
                sch.wait("vector", "nm1", t_nm)
                t_rd = sch.step("vector", lambda q, s, kt=kt: q.tensor_copy(
                    kncl[0:1, kt, :], pb[0:1, kt, 256:384]), sem="kn1d")
                sch.wait("scalar", "kn1d", t_rd)
            sch.step("vector", lambda q, s: q.drain())
            t_mx = sch.step("vector", lambda q, s: q.reduce_max(
                out=kn1sb[0:1, 2:3], in_=kncl[0:1, :, :].rearrange("a b c -> a (b c)"),
                axis=AX.X), sem="kn1d")
            sch.wait("scalar", "kn1d", sch.cur("kn1d"))
            t_kns = sch.step("scalar", lambda q, s: q.sqrt(
                kn1sb[0:1, 3:4], kn1sb[0:1, 2:3]), sem="kn1d")
            sch.wait("scalar", "kn1d", sch.cur("kn1d"))
            sch.dma("scalar", kn1o[:, :], kn1sb[0:1, 3:4], "q1st")
            sch.wait("scalar", "q1e", sch.cur("q1e"))
            sch.dma("scalar", K1o[:, :].rearrange("(kt p) d -> p kt d", p=128),
                    obuf3[:, 0, :].rearrange("p (kt d) -> p kt d", d=128), "q1st")
            t_v1 = sch.dma("scalar", V1o[:, :].rearrange("(kt p) d -> p kt d", p=128),
                           obuf3[:, 1, :].rearrange("p (kt d) -> p kt d", d=128), "q1st")
            sch.wait("scalar", "q1st", sch.cur("q1st"))
            sch.wait("sync", "q1st", sch.cur("q1st"))
            sch.wait("sync", "x2st", t_x2o)
        elif do_ffn:
            for t in range(T):
                sch.dma("sync", x2o[t * 128:(t + 1) * 128, :], XR[:, t, :], "x2st")
            sch.wait("sync", "x2st", sch.cur("x2st"))
        else:
            sch.wait("sync", "ln1", sch.cur("ln1"))
            for t in range(T):
                sch.dma("sync", x2o[t * 128:(t + 1) * 128, :], XR[:, t, :], "x2st")
            sch.wait("sync", "x2st", sch.cur("x2st"))
        sch.build(nc)
    return nc

def build_DE():
    nc = bass.Bass("TRN2", target_bir_lowering=False)
    x2 = nc.dram_tensor("x2", [SL, E], F32, kind="ExternalInput").ap()
    QT1d = nc.dram_tensor("QT1", [E, SL], BF16, kind="ExternalInput").ap()
    K1d = nc.dram_tensor("K1", [S, 128], BF16, kind="ExternalInput").ap()
    V1d = nc.dram_tensor("V1", [S, 128], BF16, kind="ExternalInput").ap()
    kn1all = nc.dram_tensor("kn1all", [1, 8], F32, kind="ExternalInput").ap()
    dmd = nc.dram_tensor("dm", [8, 128, 128], BF16, kind="ExternalInput").ap()
    wo = nc.dram_tensor("wo", [E, E], F32, kind="ExternalInput").ap()
    weight = nc.dram_tensor("weight", [E, E], F32, kind="ExternalInput").ap()
    bias = nc.dram_tensor("bias", [E], F32, kind="ExternalInput").ap()
    lora_A = nc.dram_tensor("lora_A", [3, 16, E], F32, kind="ExternalInput").ap()
    lora_B = nc.dram_tensor("lora_B", [3, E, 16], F32, kind="ExternalInput").ap()
    lsi = nc.dram_tensor("lsi", [1, 4], I32, kind="ExternalInput").ap()

    out = nc.dram_tensor("out", [SL, E], F32, kind="ExternalOutput").ap()

    attn1bf = nc.dram_tensor("attn1bf", [SL, E], BF16).ap()
    x3bf = nc.dram_tensor("x3bf", [SL, E], BF16).ap()
    wbf = nc.dram_tensor("wbf", [E, E], BF16).ap()
    labf = nc.dram_tensor("labf", [48, E], BF16).ap()
    lbbf = nc.dram_tensor("lbbf", [3 * E, 16], BF16).ap()
    biasbf = nc.dram_tensor("biasbf", [1, E], BF16).ap()
    zg3bf = nc.dram_tensor("zg3bf", [SL, 128], BF16).ap()
    lbpad = nc.dram_tensor("lbpad", [E, 128], BF16).ap()

    ones_f32 = nc.inline_tensor(np.ones((128, 1), dtype=np.float32), name="ones_f32")
    onesrow_f = nc.inline_tensor(np.ones((1, 128), dtype=np.float32), name="onesrow_f")
    ones_bf = nc.inline_tensor(np.ones((1, 128), dtype=BDT), name="ones_bf")
    zeros_bf = nc.inline_tensor(np.zeros((512, 128), dtype=BDT), name="zeros_bf")

    sch = Sched()
    with ExitStack() as es0:
        pb = es0.enter_context(nc.psum_tensor("pb", [128, 8, 512], F32))
        XR = es0.enter_context(nc.sbuf_tensor("XR", [128, T, E], F32))
        msc = es0.enter_context(nc.sbuf_tensor("msc", [128, T, 32], F32))
        onesf = es0.enter_context(nc.sbuf_tensor("onesf", [128, 1], F32))
        onesrf = es0.enter_context(nc.sbuf_tensor("onesrf", [1, 128], F32))
        onesb = es0.enter_context(nc.sbuf_tensor("onesb", [1, 128], BF16))
        negBb = es0.enter_context(nc.sbuf_tensor("negBb", [128, 32], F32))
        qnr = es0.enter_context(nc.sbuf_tensor("qnr", [1, 32], F32))
        kmx = es0.enter_context(nc.sbuf_tensor("kmx", [1, 9], F32))
        ln1sc = es0.enter_context(nc.sbuf_tensor("ln1sc", [128, 8], F32))
        lnsl = es0.enter_context(nc.sbuf_tensor("lnsl", [128, T, 4], F32))
        rzs = es0.enter_context(nc.sbuf_tensor("rzs", [128, 8], F32))
        dmb = es0.enter_context(nc.sbuf_tensor("dmb", [128, 8, 128], BF16))

        sch.dma("sync", onesf[:], ones_f32.ap()[:, :], "cl")
        sch.dma("sync", onesrf[:], onesrow_f.ap()[:, :], "cl")
        sch.dma("sync", onesb[:], ones_bf.ap()[:, :], "cl")
        sch.dma("sync", kmx[0:1, 0:8], kn1all[:, :], "cl")
        sch.dma("sync", dmb[:, :, :], dmd.rearrange("r k q -> k r q")[:, :, :], "cl")
        for t in range(T):
            sch.dma("sync", XR[:, t, :], x2[t * 128:(t + 1) * 128, :], "xld")
        t_xld = sch.cur("xld")
        sch.dma("gpsimd", wbf[:, :], weight[:, :], "wcst")
        sch.dma("gpsimd", labf[:, :], lora_A.rearrange("j r e -> (j r) e")[:, :], "wcst")
        sch.dma("gpsimd", lbbf[:, :], lora_B.rearrange("j e r -> (j e) r")[:, :], "wcst")
        for zz in range(8):
            sch.dma("gpsimd", lbpad[zz * 512:(zz + 1) * 512, :], zeros_bf.ap()[:, :], "wcst")
        sch.wait("gpsimd", "wcst", sch.cur("wcst"))
        for j in range(3):
            sch.dma("gpsimd", lbpad[:, j * 16:(j + 1) * 16], lbbf[j * E:(j + 1) * E, :], "wcst")
        sch.dma("gpsimd", biasbf[:, :], bias.rearrange("(a e) -> a e", a=1)[:, :], "wcst")

        def pbq(t):
            return lnsl[:, t, 0:1]
        def pbs(t):
            return lnsl[:, t, 1:2]
        def pbr(t):
            return lnsl[:, t, 2:3]
        def pbm(t):
            return lnsl[:, t, 3:4]

        def emit_ln(tile, ncols, lnsem, junk):
            sch.step("vector", lambda q, s, tile=tile, ncols=ncols: q.reduce_sum(
                out=ln1sc[:, tile:tile + 1], in_=msc[:, tile, 0:ncols], axis=AX.X))
            sch.step("vector", lambda q, s: q.drain())
            t_nm = sch.step("vector", lambda q, s, tile=tile: q.tensor_scalar(
                out=ln1sc[:, tile:tile + 1], in0=ln1sc[:, tile:tile + 1],
                scalar1=-1.0 / E, scalar2=None, op0=ALU.mult), sem=lnsem)
            sch.wait("scalar", lnsem, t_nm)
            t_sq = sch.step("scalar", lambda q, s, tile=tile, junk=junk: q.activation(
                out=junk, in_=XR[:, tile, :], func=AF.Square,
                bias=ln1sc[:, tile:tile + 1], accum_out=pbq(tile)), sem=lnsem)
            sch.wait("vector", lnsem, t_sq)
            t_ve = sch.step("vector", lambda q, s, tile=tile: q.tensor_scalar(
                out=pbq(tile), in0=pbq(tile), scalar1=1.0 / E, scalar2=LN_EPS,
                op0=ALU.mult, op1=ALU.add), sem=lnsem)
            sch.wait("scalar", lnsem, t_ve)
            t_va = sch.step("scalar", lambda q, s, tile=tile: q.activation(
                out=pbs(tile), in_=pbq(tile), func=AF.Sqrt), sem=lnsem)
            sch.wait("vector", lnsem, t_va)
            sch.step("vector", lambda q, s: q.drain())
            sch.step("vector", lambda q, s, tile=tile: q.reciprocal(
                out=pbr(tile), in_=pbs(tile)))
            sch.step("vector", lambda q, s: q.drain())
            sch.step("vector", lambda q, s, tile=tile: q.scalar_tensor_tensor(
                out=pbm(tile), in0=ln1sc[:, tile:tile + 1], scalar=1.0,
                in1=pbr(tile), op0=ALU.mult, op1=ALU.mult))
            sch.step("vector", lambda q, s: q.drain())
            t_ln = sch.step("vector", lambda q, s, tile=tile: q.scalar_tensor_tensor(
                out=XR[:, tile, :], in0=XR[:, tile, :], scalar=pbr(tile),
                in1=pbm(tile).to_broadcast([128, E]), op0=ALU.mult, op1=ALU.add), sem=lnsem)
            return t_ln

        esD1 = es0.enter_context(ExitStack())
        K1Tb = esD1.enter_context(nc.sbuf_tensor("K1Tb", [128, S], BF16))
        V1b = esD1.enter_context(nc.sbuf_tensor("V1b", [128, 64, 130], BF16))
        QTh = esD1.enter_context(nc.sbuf_tensor("QTh", [128, 2, SL], BF16))
        Esb = esD1.enter_context(nc.sbuf_tensor("Esb", [128, 2, 512], BF16))
        attnstg = esD1.enter_context(nc.sbuf_tensor("attnstg", [128, SL], BF16))
        qnb = esD1.enter_context(nc.sbuf_tensor("qnb", [128, 2, SL], BF16))
        sqs = esD1.enter_context(nc.sbuf_tensor("sqs", [128, SL], F32))

        for kh in range(2):
            sch.dma_t("sync", K1Tb[:, kh * 4096:(kh + 1) * 4096],
                      K1d[kh * 4096:(kh + 1) * 4096, :], "kvl")
        for part in range(8):
            sch.dma("sync", V1b[:, part * 8:(part + 1) * 8, 0:128],
                    V1d[part * 1024:(part + 1) * 1024, :].rearrange("(kc p) d -> p kc d", p=128),
                    "kvl")
        t_kv = sch.cur("kvl")
        sch.wait("vector", "kvl", t_kv)
        t_vo = sch.step("vector", lambda q, s: q.memset(V1b[:, :, 128:129], 1.0), sem="vone")

        for h in range(32):
            pp = h % 2
            if h >= 2:
                sch.wait("sync", "qnf", h - 1)
            t_l = sch.dma("sync", qnb[:, pp, :], QT1d[h * 128:(h + 1) * 128, :], "qnl")
            sch.wait("scalar", "qnl", t_l)
            t_s = sch.step("scalar", lambda q, s, pp=pp: q.square(
                sqs[:, :], qnb[:, pp, :]), sem="qns")
            sch.wait("tensor", "qns", t_s)
            for half in range(2):
                sch.step("tensor", lambda q, s, half=half: q.matmul(
                    pb[0:1, half, :], onesf[:], sqs[:, half * 512:(half + 1) * 512],
                    start=True, stop=True), sem="qnm" if half == 1 else None)
            sch.wait("vector", "qnm", sch.cur("qnm"))
            sch.step("vector", lambda q, s: q.reduce_max(
                out=lnsl[0:1, 0, 0:1], in_=pb[0:1, 0, :], axis=AX.X))
            sch.step("vector", lambda q, s: q.reduce_max(
                out=lnsl[0:1, 0, 1:2], in_=pb[0:1, 1, :], axis=AX.X))
            sch.step("vector", lambda q, s: q.drain())
            t_f = sch.step("vector", lambda q, s, h=h: q.reduce_max(
                out=qnr[0:1, h:h + 1], in_=lnsl[0:1, 0, 0:2], axis=AX.X), sem="qnf")
            sch.wait("tensor", "qnf", t_f)
        sch.wait("scalar", "qnf", sch.cur("qnf"))
        t_sq2 = sch.step("scalar", lambda q, s: q.sqrt(qnr[:, :], qnr[:, :]), sem="qns")
        sch.wait("vector", "qns", t_sq2)
        sch.wait("vector", "cl", sch.cur("cl"))
        sch.step("vector", lambda q, s: q.reduce_max(
            out=kmx[0:1, 8:9], in_=kmx[0:1, 0:8], axis=AX.X))
        sch.step("vector", lambda q, s: q.drain())
        t_nb = sch.step("vector", lambda q, s: q.scalar_tensor_tensor(
            out=qnr[:, :], in0=qnr[:, :], scalar=-RSQD,
            in1=kmx[0:1, 8:9].to_broadcast([1, 32]), op0=ALU.mult, op1=ALU.mult), sem="qnf")
        sch.wait("tensor", "qnf", sch.cur("qnf"))
        t_bb = sch.step("tensor", lambda q, s: q.matmul(
            pb[:, 0, 0:32], onesrf[:], qnr[:, :], start=True, stop=True), sem="qnm")
        sch.wait("vector", "qnm", sch.cur("qnm"))
        t_ev = sch.step("vector", lambda q, s: q.tensor_copy(negBb[:, :], pb[:, 0, 0:32]), sem="qnf")
        sch.wait("tensor", "qnf", sch.cur("qnf"))
        sch.wait("scalar", "qnf", sch.cur("qnf"))

        steps = []
        for b in range(8):
            w = (8 - b) * 128
            q0 = b * 128
            segs = []
            while w > 0:
                sw = min(512, w)
                segs.append((q0, sw))
                q0 += sw
                w -= sw
            for r in range(8):
                for (qq, sw) in segs:
                    steps.append((b, r, qq, sw))
        NST = len(steps)
        exw = []
        tot = 0
        for (b, r, qq, sw) in steps:
            tot += 1 + (1 if qq == b * 128 else 0)
            exw.append(tot)

        ACS = 132
        def pacc(i):
            return pb[:, 4 + i // 3, (i % 3) * ACS:(i % 3) * ACS + 129]

        for h in range(32):
            pp = h % 2
            if h >= 2:
                sch.wait("sync", "a1d", 16 * (h - 1))
            t_q = sch.dma("sync", QTh[:, pp, :], QT1d[h * 128:(h + 1) * 128, :], "qhl")
            base_ex = sch.cur("ex1")
            base_av = sch.cur("av1")
            base_sm = sch.cur("sm1")
            for i in range(NST + 1):
                if i < NST:
                    b, r, q0, sw = steps[i]
                    kc = r * 8 + b
                    if i == 0:
                        sch.wait("tensor", "qhl", t_q)
                        sch.wait("tensor", "vone", sch.cur("vone"))
                        sch.wait("tensor", "ev1", sch.cur("ev1"))
                        sch.wait("tensor", "qnf", sch.cur("qnf"))
                    if i >= 2:
                        sch.wait("tensor", "ex1", base_ex + exw[i - 2])
                    sch.step("tensor", lambda q, s, kc=kc, q0=q0, sw=sw, i=i, pp=pp: q.matmul(
                        pb[:, i % 2, 0:sw], K1Tb[:, kc * 128:(kc + 1) * 128],
                        QTh[:, pp, q0:q0 + sw], start=True, stop=True), sem="sm1")
                if i >= 1:
                    b, r, q0, sw = steps[i - 1]
                    kc = r * 8 + b
                    sch.wait("tensor", "ex1", base_ex + exw[i - 1])
                    nt = sw // 128
                    for j in range(nt):
                        t_idx = q0 // 128 + j
                        last = (b == t_idx and r == 7)
                        first = (b == 0 and r == 0)
                        sch.step("tensor", lambda q, s, i=i, j=j, kc=kc, t_idx=t_idx, last=last, first=first: q.matmul(
                            pacc(t_idx), Esb[:, (i - 1) % 2, j * 128:(j + 1) * 128],
                            V1b[:, kc, 0:129],
                            start=(first and t_idx % 3 == 0), stop=last,
                            skip_group_check=True),
                            sem="av1" if j == nt - 1 else None)
            for i in range(NST):
                b, r, q0, sw = steps[i]
                diag = (q0 == b * 128)
                if i >= 2:
                    sch.wait("scalar", "av1", base_av + i - 1)
                elif i == 0 and h > 0:
                    sch.wait("scalar", "av1", base_av)
                sch.wait("scalar", "sm1", base_sm + i + 1)
                t_e = sch.step("scalar", lambda q, s, i=i, sw=sw, h=h: q.activation(
                    out=Esb[:, i % 2, 0:sw], in_=pb[:, i % 2, 0:sw],
                    func=AF.Exp, bias=negBb[:, h:h + 1], scale=RSQD), sem="ex1")
                if diag:
                    sch.wait("vector", "ex1", t_e)
                    sch.step("vector", lambda q, s, i=i, r=r: q.tensor_tensor(
                        out=Esb[:, i % 2, 0:128], in0=Esb[:, i % 2, 0:128],
                        in1=dmb[:, r, :], op=ALU.mult), sem="ex1")
            sch.wait("vector", "av1", base_av + NST)
            sch.step("vector", lambda q, s: q.drain())
            for t_i in range(T):
                sch.step("vector", lambda q, s, t_i=t_i: q.reciprocal(
                    out=rzs[:, t_i:t_i + 1], in_=pacc(t_i)[:, 128:129]))
            sch.step("vector", lambda q, s: q.drain())
            for t_i in range(T):
                sch.step("vector", lambda q, s, t_i=t_i: q.tensor_scalar(
                    out=attnstg[:, t_i * 128:(t_i + 1) * 128], in0=pacc(t_i)[:, 0:128],
                    scalar1=rzs[:, t_i:t_i + 1], scalar2=None, op0=ALU.mult), sem="ev1")
            t_ev1 = sch.cur("ev1")
            sch.wait("sync", "ev1", t_ev1)
            t_a1 = sch.dma("sync", attn1bf[:, h * 128:(h + 1) * 128].rearrange("(qt p) d -> p qt d", p=128),
                           attnstg[:, :].rearrange("p (qt d) -> p qt d", d=128), "a1d")
            sch.wait("vector", "a1d", t_a1)
        esD1.close()

        esD2 = es0.enter_context(ExitStack())
        a1T = esD2.enter_context(nc.sbuf_tensor("a1T", [128, EC, SL], BF16))
        wt = esD2.enter_context(nc.sbuf_tensor("wt", [128, 2, 512], BF16))
        sch.wait("sync", "a1d", 16 * 32)
        for c in range(EC):
            sch.dma_t("sync", a1T[:, c, :], attn1bf[:, c * 128:(c + 1) * 128], "a1tl")
        t_a1t = sch.cur("a1tl")
        sch.wait("gpsimd", "a1d", 16 * 32)
        for o in range(8):
            ybase = sch.cur("womf")
            for ec in range(EC):
                step = o * 32 + ec
                pp = step % 2
                if ec >= 2:
                    sch.wait("gpsimd", "womf", ybase + ec - 1)
                elif o > 0:
                    sch.wait("gpsimd", "womf", ybase)
                t_w = sch.dma("gpsimd", wt[:, pp, :],
                              wo[ec * 128:(ec + 1) * 128, o * 512:(o + 1) * 512], "wol")
                sch.wait("tensor", "wol", t_w)
                if step == 0:
                    sch.wait("tensor", "a1tl", t_a1t)
                if ec == 0:
                    sch.wait("tensor", "ev3", 8 * o)
                for qt in range(T):
                    sch.step("tensor", lambda q, s, qt=qt, ec=ec, pp=pp: q.matmul(
                        pb[:, qt, :], a1T[:, ec, qt * 128:(qt + 1) * 128], wt[:, pp, :],
                        start=(ec == 0), stop=(ec == EC - 1)),
                        sem="womf" if qt == T - 1 else None)
            sch.wait("vector", "womf", sch.cur("womf"))
            if o == 0:
                sch.wait("vector", "xld", t_xld)
            for qt in range(T):
                sch.step("vector", lambda q, s, qt=qt, o=o: q.scalar_tensor_tensor(
                    out=XR[:, qt, o * 512:(o + 1) * 512], in0=pb[:, qt, :],
                    scalar=1.0, in1=XR[:, qt, o * 512:(o + 1) * 512],
                    op0=ALU.mult, op1=ALU.add,
                    accum_out=msc[:, qt, o:o + 1]), sem="ev3")
        for t in range(T):
            emit_ln(t, 8, "ln3", a1T[:, 0:4, :].rearrange("p c q -> p (c q)"))
        sch.wait("vector", "ln3", sch.cur("ln3"))

        stg = a1T[:, 0:4, :].rearrange("p c q -> p (c q)")
        for t in range(T):
            t_c = sch.step("vector", lambda q, s, t=t, stg=stg: q.tensor_copy(
                stg[:, 0:E], XR[:, t, :]), sem="xc3")
            sch.wait("sync", "xc3", t_c)
            t_s = sch.dma("sync", x3bf[t * 128:(t + 1) * 128, :], stg[:, 0:E], "x3st")
            sch.wait("vector", "x3st", t_s)
        sch.wait("sync", "x3st", sch.cur("x3st"))
        for c in range(EC):
            sch.dma_t("sync", a1T[:, c, :], x3bf[:, c * 128:(c + 1) * 128], "x3tl")
        t_x3t = sch.cur("x3tl")

        esD3 = es0.enter_context(ExitStack())
        A3T = esD3.enter_context(nc.sbuf_tensor("A3T", [128, 2, 48], BF16))
        zg3T = esD3.enter_context(nc.sbuf_tensor("zg3T", [128, SL], BF16))
        BT3 = esD3.enter_context(nc.sbuf_tensor("BT3", [128, 2, 512], BF16))
        zg3sb = esD3.enter_context(nc.sbuf_tensor("zg3sb", [128, 2, 128], BF16))
        cbc = esD3.enter_context(nc.sbuf_tensor("cbc", [128, 4], F32))
        lsib = esD3.enter_context(nc.sbuf_tensor("lsib", [1, 4], I32))
        lsf = esD3.enter_context(nc.sbuf_tensor("lsf", [1, 12], F32))
        bsl = esD3.enter_context(nc.sbuf_tensor("bsl", [1, 2, 512], BF16))

        t_zz = sch.step("vector", lambda q, s: q.memset(zg3sb[:, :, 48:128], 0.0), sem="zg3")
        sch.wait("sync", "wcst", sch.cur("wcst"))
        t_ls = sch.dma("sync", lsib[:], lsi[:, :], "lsl")
        sch.wait("vector", "lsl", t_ls)
        sch.step("vector", lambda q, s: q.tensor_copy(lsf[0:1, 0:4], lsib[:]))
        sch.step("vector", lambda q, s: q.drain())
        sch.step("vector", lambda q, s: q.tensor_scalar(
            out=lsf[0:1, 4:8], in0=lsf[0:1, 0:4], scalar1=0.0, scalar2=None, op0=ALU.is_le))
        sch.step("vector", lambda q, s: q.drain())
        sch.step("vector", lambda q, s: q.tensor_tensor(
            out=lsf[0:1, 10:11], in0=lsf[0:1, 6:7], in1=lsf[0:1, 7:8], op=ALU.mult))
        sch.step("vector", lambda q, s: q.drain())
        sch.step("vector", lambda q, s: q.tensor_sub(
            out=lsf[0:1, 9:10], in0=lsf[0:1, 6:7], in1=lsf[0:1, 10:11]))
        sch.step("vector", lambda q, s: q.tensor_scalar(
            out=lsf[0:1, 11:12], in0=lsf[0:1, 6:7], scalar1=-1.0, scalar2=1.0,
            op0=ALU.mult, op1=ALU.add))
        sch.step("vector", lambda q, s: q.drain())
        t_c8 = sch.step("vector", lambda q, s: q.tensor_tensor(
            out=lsf[0:1, 8:9], in0=lsf[0:1, 5:6], in1=lsf[0:1, 11:12], op=ALU.mult), sem="lsd")
        sch.wait("tensor", "lsd", t_c8)
        t_cb = sch.step("tensor", lambda q, s: q.matmul(
            pb[:, 0, 0:3], onesrf[:], lsf[0:1, 8:11], start=True, stop=True), sem="lsm")
        sch.wait("vector", "lsm", t_cb)
        t_cbe = sch.step("vector", lambda q, s: q.tensor_copy(cbc[:, 0:3], pb[:, 0, 0:3]), sem="lsd")

        sch.wait("tensor", "x3tl", t_x3t)
        sch.wait("tensor", "lsd", sch.cur("lsd"))
        for ec in range(EC):
            pp = ec % 2
            if ec >= 2:
                sch.wait("sync", "z3m", ec - 1)
            t_a3 = sch.dma_t("sync", A3T[:, pp, :], labf[0:48, ec * 128:(ec + 1) * 128], "a3l")
            sch.wait("tensor", "a3l", t_a3)
            for qt in range(T):
                sch.step("tensor", lambda q, s, qt=qt, ec=ec, pp=pp: q.matmul(
                    pb[:, 1, qt * 48:(qt + 1) * 48], a1T[:, ec, qt * 128:(qt + 1) * 128],
                    A3T[:, pp, :], start=(ec == 0 and qt == 0), stop=(ec == EC - 1),
                    skip_group_check=True), sem="z3m" if qt == T - 1 else None)
        sch.wait("vector", "z3m", sch.cur("z3m"))
        sch.wait("vector", "lsd", sch.cur("lsd"))
        for qt in range(T):
            for j in range(3):
                sch.step("vector", lambda q, s, qt=qt, j=j: q.tensor_scalar(
                    out=zg3sb[:, qt % 2, j * 16:(j + 1) * 16],
                    in0=pb[:, 1, qt * 48 + j * 16:qt * 48 + (j + 1) * 16],
                    scalar1=cbc[:, j:j + 1], scalar2=None, op0=ALU.mult),
                    sem="zg3" if j == 2 else None)
            t_z = sch.cur("zg3")
            sch.wait("sync", "zg3", t_z)
            t_zs = sch.dma("sync", zg3bf[qt * 128:(qt + 1) * 128, :], zg3sb[:, qt % 2, :], "zg3s")
            if qt >= 1:
                sch.wait("vector", "zg3s", t_zs - 16)
        sch.wait("sync", "zg3s", sch.cur("zg3s"))
        t_zgt = sch.dma_t("sync", zg3T[:, :], zg3bf[:, :], "zgtl")

        for o in range(8):
            ppo = o % 2
            if o >= 2:
                sch.wait("sync", "ymf2", o - 1)
            sch.dma_t("sync", BT3[:, ppo, :], lbpad[o * 512:(o + 1) * 512, :], "btl3")
            sch.dma("sync", bsl[0:1, ppo, :], biasbf[0:1, o * 512:(o + 1) * 512], "btl3")
            t_bt3 = sch.cur("btl3")
            ybase = sch.cur("ymf")
            for ec in range(EC):
                pp = (o * 32 + ec) % 2
                if ec >= 2:
                    sch.wait("sync", "ymf", ybase + ec - 1)
                elif o > 0:
                    sch.wait("sync", "ymf", ybase)
                t_w = sch.dma_t("sync", wt[:, pp, :],
                                wbf[o * 512:(o + 1) * 512, ec * 128:(ec + 1) * 128], "wyl")
                sch.wait("tensor", "wyl", t_w)
                if ec == 0:
                    sch.wait("tensor", "ev4", 8 * o)
                    sch.wait("tensor", "zgtl", sch.cur("zgtl"))
                    sch.wait("tensor", "btl3", t_bt3)
                for qt in range(T):
                    sch.step("tensor", lambda q, s, qt=qt, ec=ec, pp=pp: q.matmul(
                        pb[:, qt, :], a1T[:, ec, qt * 128:(qt + 1) * 128], wt[:, pp, :],
                        start=(ec == 0), stop=False),
                        sem="ymf" if qt == T - 1 else None)
            for qt in range(T):
                sch.step("tensor", lambda q, s, qt=qt, ppo=ppo: q.matmul(
                    pb[:, qt, :], zg3T[:, qt * 128:(qt + 1) * 128], BT3[:, ppo, :],
                    start=False, stop=False))
                sch.step("tensor", lambda q, s, qt=qt, ppo=ppo: q.matmul(
                    pb[:, qt, :], onesb[:], bsl[0:1, ppo, :],
                    start=False, stop=True), sem="ymf2" if qt == T - 1 else None)
            sch.wait("vector", "ymf2", sch.cur("ymf2"))
            for qt in range(T):
                sch.step("vector", lambda q, s, qt=qt, o=o: q.scalar_tensor_tensor(
                    out=XR[:, qt, o * 512:(o + 1) * 512], in0=pb[:, qt, :],
                    scalar=1.0, in1=XR[:, qt, o * 512:(o + 1) * 512],
                    op0=ALU.mult, op1=ALU.add,
                    accum_out=msc[:, qt, o:o + 1]), sem="ev4")
        for t in range(T):
            emit_ln(t, 8, "ln4", a1T[:, 4:8, :].rearrange("p c q -> p (c q)"))
        sch.wait("sync", "ln4", sch.cur("ln4"))
        for t in range(T):
            sch.dma("sync", out[t * 128:(t + 1) * 128, :], XR[:, t, :], "ost")
        sch.wait("gpsimd", "ost", sch.cur("ost"))
        sch.build(nc)
    return nc


def make_diagmask(i):
    jk = np.arange(128)[:, None]
    jq = np.arange(128)[None, :]
    m = np.zeros((8, 128, 128), dtype=BDT)
    for r in range(8):
        m[r] = ((8 * jk + r) <= (8 * jq + i)).astype(BDT)
    return m


def make_inmaps_DE(inputs, resBC):
    K1_full = np.concatenate([np.asarray(r["K1"]) for r in resBC], axis=0)
    V1_full = np.concatenate([np.asarray(r["V1"]) for r in resBC], axis=0)
    kn1all = np.stack([np.asarray(r["kn1"]).ravel() for r in resBC], 0).reshape(1, 8)
    in_maps = []
    for i in range(NCORE):
        in_maps.append({
            "x2": np.asarray(resBC[i]["x2"]),
            "QT1": np.asarray(resBC[i]["QT1"]),
            "K1": K1_full,
            "V1": V1_full,
            "kn1all": kn1all,
            "dm": make_diagmask(i),
            "wo": np.asarray(inputs["L1_attn_wo"]),
            "weight": np.asarray(inputs["L1_ffn_weight"]),
            "bias": np.asarray(inputs["L1_ffn_bias"]),
            "lora_A": np.asarray(inputs["L1_ffn_lora_A"]),
            "lora_B": np.asarray(inputs["L1_ffn_lora_B"]),
            "lsi": np.asarray(inputs["L1_ffn_lora_start_indices"], np.int32).reshape(1, 4),
        })
    return in_maps


def run_DE(inputs, resBC):
    nc = build_DE()
    res = run_bass_kernel_spmd(nc, make_inmaps_DE(inputs, resBC), core_ids=list(range(NCORE)))
    return res.results


def make_inmaps_BC(inputs, resA):
    x = np.asarray(inputs["L0_attn_x"])[0]
    K0T_full = np.concatenate([np.asarray(r["K0T"]) for r in resA], axis=1)
    V0_full = np.concatenate([np.asarray(r["V0"]) for r in resA], axis=1)
    knall = np.stack([np.asarray(r["kn"]).ravel() for r in resA], 0).reshape(1, 64)
    in_maps = []
    for i in range(NCORE):
        in_maps.append({
            "x": _shard_rows(x, i),
            "QT0": np.asarray(resA[i]["QT0"]),
            "K0T": K0T_full,
            "V0": V0_full,
            "knall": knall,
            "A_w": np.asarray(inputs["L0_ffn_A_w"]),
            "B_w": np.asarray(inputs["L0_ffn_B_w"]),
            "Wq_w": np.asarray(inputs["L0_ffn_Wq_w"]),
            "keys": np.asarray(inputs["L0_ffn_keys"]),
            "wq1": np.asarray(inputs["L1_attn_wq"]),
            "wk1": np.asarray(inputs["L1_attn_wk"]),
            "wv1": np.asarray(inputs["L1_attn_wv"]),
        })
    return in_maps


def run_BC(inputs, resA, **bkw):
    nc = build_BC(**bkw)
    res = run_bass_kernel_spmd(nc, make_inmaps_BC(inputs, resA), core_ids=list(range(NCORE)))
    return res.results


def kernel(**inputs):
    raise NotImplementedError("under construction")


def kernel(**inputs):
    resA = run_A(inputs)
    resBC = run_BC(inputs, resA)
    resDE = run_DE(inputs, resBC)
    shards = [np.asarray(r["out"], np.float32) for r in resDE]
    out = np.stack(shards, axis=1).reshape(S, E)
    return out[None]
